# version 06
# baseline (speedup 1.0000x reference)
import math
from contextlib import ExitStack

import numpy as np
import ml_dtypes
import concourse.bass as bass
import concourse.mybir as mybir
from concourse.bass_utils import run_bass_kernel_spmd

F32 = mybir.dt.float32
BF16 = mybir.dt.bfloat16
I32 = mybir.dt.int32
AF = mybir.ActivationFunctionType
ALU = mybir.AluOpType
AX = mybir.AxisListType

LN2 = 0.6931471805599453
INV2LN2 = 1.0 / (2.0 * LN2)
QSCALE_LN = math.log(512.0 ** -0.5)

R = 2048
NT = 16
NT_OWN = 8
D = 2048
NKB = 16
FI = 8192
NIC = 16
NKI = 64
H = 4
NCH = 16
NG = 4
EPS = 1e-5

C_ZERO, C_ONE, C_BSTEP, C_BISTEP, C_BSTEPH, C_BISTEPH, C_EPS, C_QSC = range(8)
CONST_VALS = [0.0, 1.0, -67.0 * LN2, 67.0 * LN2, -9.0 * LN2, 9.0 * LN2, EPS, QSCALE_LN]


class Prog:
    def __init__(self):
        self.items = []
        self.cnt = {}

    def now(self, s):
        return self.cnt.get(s, 0)

    def op(self, eng, fn, inc=None, waits=()):
        self.items.append((eng, fn, inc, tuple(waits)))
        if inc is not None:
            self.cnt[inc[0]] = self.cnt.get(inc[0], 0) + inc[1]

    def dma(self, out, in_, waits=()):
        self.op('sync', lambda e, out=out, in_=in_: e.dma_start(out=out, in_=in_),
                inc=('dm', 16), waits=waits)
        self.op('sync', lambda e: None, waits=self.wdm())

    def dmat(self, out, in_, waits=()):
        self.op('sync', lambda e, out=out, in_=in_: e.dma_start_transpose(out, in_),
                inc=('dm', 16), waits=tuple(waits))
        self.op('sync', lambda e: None, waits=self.wdm())

    def v(self, fn, inc=None, waits=()):
        self.op('vector', fn, inc=inc, waits=waits)

    def s(self, fn, inc=None, waits=()):
        self.op('scalar', fn, inc=inc, waits=waits)

    def t(self, fn, inc=None, waits=()):
        self.op('tensor', fn, inc=inc, waits=waits)

    def wdm(self):
        return (('dm', self.now('dm')),)

    def wv(self):
        return (('v', self.now('v')),)

    def ws(self):
        return (('s', self.now('s')),)

    def wt(self):
        return (('t', self.now('t')),)


def build_program():
    nc = bass.Bass("TRN2", target_bir_lowering=False, debug=False)

    x_in = nc.dram_tensor("x_in", [R, D], F32, kind="ExternalInput").ap()
    wq = nc.dram_tensor("wq", [D, D], F32, kind="ExternalInput").ap()
    wk = nc.dram_tensor("wk", [D, D], F32, kind="ExternalInput").ap()
    wv = nc.dram_tensor("wv", [D, 1024], F32, kind="ExternalInput").ap()
    wgk1p = nc.dram_tensor("wgk1p", [D, 128], F32, kind="ExternalInput").ap()
    wgk2p = nc.dram_tensor("wgk2p", [128, D], F32, kind="ExternalInput").ap()
    bgk2l = nc.dram_tensor("bgk2l", [128, 16], F32, kind="ExternalInput").ap()
    wg = nc.dram_tensor("wg", [D, 1024], F32, kind="ExternalInput").ap()
    wnormt = nc.dram_tensor("wnormt", [128, 8], F32, kind="ExternalInput").ap()
    wo = nc.dram_tensor("wo", [1024, D], F32, kind="ExternalInput").ap()
    gate_w = nc.dram_tensor("gate_w", [FI, D], F32, kind="ExternalInput").ap()
    up_w = nc.dram_tensor("up_w", [FI, D], F32, kind="ExternalInput").ap()
    down_w = nc.dram_tensor("down_w", [D, FI], F32, kind="ExternalInput").ap()
    maskt_d = nc.dram_tensor("maskt", [128, 128], F32, kind="ExternalInput").ap()
    ident_d = nc.dram_tensor("ident", [128, 128], BF16, kind="ExternalInput").ap()
    y_out = nc.dram_tensor("y_out", [1024, D], F32, kind="ExternalOutput").ap()

    dqwTg = nc.dram_tensor("dqwTg", [NIC, 128, NKB * 512], BF16).ap()
    dqwTu = nc.dram_tensor("dqwTu", [NIC, 128, NKB * 512], BF16).ap()
    dqwTd = nc.dram_tensor("dqwTd", [4, 128, NKI * 512], BF16).ap()
    dqhT = nc.dram_tensor("dqhT", [NT, 128, NKI * 128], BF16).ap()
    x1d = nc.dram_tensor("x1d", [NT, 128, D], F32).ap()
    x2d = nc.dram_tensor("x2d", [NT_OWN, 128, D], F32).ap()
    wq16 = nc.dram_tensor("wq16", [NKB, 128, D], BF16).ap()
    wk16 = nc.dram_tensor("wk16", [NKB, 128, D], BF16).ap()
    wv16 = nc.dram_tensor("wv16", [NKB, 128, 1024], BF16).ap()
    wg16 = nc.dram_tensor("wg16", [NKB, 128, 1024], BF16).ap()
    woN16 = nc.dram_tensor("woN16", [8, 128, D], BF16).ap()
    wgk1p16 = nc.dram_tensor("wgk1p16", [NKB, 128, 128], BF16).ap()
    wgk2p16 = nc.dram_tensor("wgk2p16", [1, 128, D], BF16).ap()

    P = Prog()

    with ExitStack() as st_:
        def sbt(name, shape, dt=F32):
            return st_.enter_context(nc.sbuf_tensor(name, shape, dt))

        cst = sbt("cst", [128, 16])
        wnt = sbt("wnt", [128, 8])
        nbg = sbt("nbg", [128, 16])
        maskt = sbt("maskt_s", [128, 128])
        ident = sbt("ident_s", [128, 128], BF16)
        SC1 = sbt("SC1", [128, 2048])
        SC2 = sbt("SC2", [128, 2048])
        SC3 = sbt("SC3", [128, 2048])
        SC4 = sbt("SC4", [128, 2048])
        SCI = sbt("SCI", [128, 2048], I32)
        SCB1 = sbt("SCB1", [128, 2048], BF16)
        SCB2 = sbt("SCB2", [128, 2048], BF16)
        BIGA = sbt("BIGA", [128, 32768], BF16)
        BIGB = sbt("BIGB", [128, 8192])
        BIGC = sbt("BIGC", [128, 16384], BF16)
        SST = sbt("SST", [128, 4096])
        SB16 = sbt("SB16", [128, 4096], BF16)
        mblk = sbt("mblk", [128, 64])
        sblk = sbt("sblk", [128, 64])
        invblk = sbt("invblk", [128, 64])
        tiny1 = sbt("tiny1", [128, 8])
        tiny2 = sbt("tiny2", [128, 8])
        t1sb = sbt("t1sb", [128, 512], BF16)
        PB = [st_.enter_context(nc.psum_tensor(f"PB{i}", [128, 512], F32))
              for i in range(8)]

        sems = {}
        for sname in ['dm', 'v', 's', 't']:
            sems[sname] = st_.enter_context(nc.semaphore(sname))

        def quant_block_scales(src3, nb):
            P.v(lambda e: e.tensor_reduce(out=mblk[:, 0:nb], in_=src3, axis=AX.X,
                                          op=ALU.max, apply_absolute_value=True),
                waits=P.wdm())
            P.v(lambda e: e.tensor_scalar(out=sblk[:, 0:nb], in0=mblk[:, 0:nb],
                                          scalar1=448e-30, scalar2=1.0 / 448.0,
                                          op0=ALU.max, op1=ALU.mult))
            P.v(lambda e: e.reciprocal(out=invblk[:, 0:nb], in_=sblk[:, 0:nb]))

        def quant_cheap(src_ap, n, out_bf16_ap):
            nb = n // 32
            src3 = src_ap.rearrange("p (b k) -> p b k", k=32)
            quant_block_scales(src3, nb)
            inv3 = invblk[:, 0:nb].unsqueeze(2).broadcast_to([128, nb, 32])
            s3 = sblk[:, 0:nb].unsqueeze(2).broadcast_to([128, nb, 32])
            y3 = SC1[:, 0:n].rearrange("p (b k) -> p b k", k=32)
            P.v(lambda e: e.tensor_tensor(out=y3, in0=src3, in1=inv3, op=ALU.mult))
            y = SC1[:, 0:n]
            P.v(lambda e: e.tensor_tensor(out=SC2[:, 0:n], in0=y, in1=y, op=ALU.mult))
            P.v(lambda e: e.tensor_single_scalar(out=SC2[:, 0:n], in_=SC2[:, 0:n],
                                                 scalar=2.0 ** -12, op=ALU.max),
                inc=('v', 1))
            P.s(lambda e: e.activation(out=SC2[:, 0:n], in_=SC2[:, 0:n], func=AF.Ln,
                                       bias=cst[:, C_ZERO:C_ZERO + 1], scale=1.0),
                inc=('s', 1), waits=P.wv())
            P.v(lambda e: e.tensor_scalar(out=SCI[:, 0:n], in0=SC2[:, 0:n],
                                          scalar1=INV2LN2, scalar2=63.5,
                                          op0=ALU.mult, op1=ALU.add),
                waits=P.ws())
            P.v(lambda e: e.tensor_scalar(out=SC2[:, 0:n], in0=SCI[:, 0:n],
                                          scalar1=58.0, scalar2=72.0,
                                          op0=ALU.max, op1=ALU.min),
                inc=('v', 1))
            P.s(lambda e: e.activation(out=SC3[:, 0:n], in_=SC2[:, 0:n], func=AF.Exp,
                                       bias=cst[:, C_BSTEP:C_BSTEP + 1], scale=LN2),
                waits=P.wv())
            P.s(lambda e: e.activation(out=SC2[:, 0:n], in_=SC2[:, 0:n], func=AF.Exp,
                                       bias=cst[:, C_BISTEP:C_BISTEP + 1], scale=-LN2),
                inc=('s', 1))
            P.v(lambda e: e.tensor_tensor(out=SC1[:, 0:n], in0=SC1[:, 0:n],
                                          in1=SC2[:, 0:n], op=ALU.mult),
                waits=P.ws())
            P.v(lambda e: e.tensor_single_scalar(out=SCI[:, 0:n], in_=SC1[:, 0:n],
                                                 scalar=15.75, op=ALU.add))
            P.v(lambda e: e.tensor_copy(SC2[:, 0:n], SCI[:, 0:n]))
            P.v(lambda e: e.scalar_tensor_tensor(out=SC4[:, 0:n], in0=SC1[:, 0:n],
                                                 scalar=16.5, in1=SC2[:, 0:n],
                                                 op0=ALU.add, op1=ALU.subtract))
            P.v(lambda e: e.tensor_single_scalar(out=SC4[:, 0:n], in_=SC4[:, 0:n],
                                                 scalar=1.0, op=ALU.is_ge))
            P.v(lambda e: e.tensor_tensor(out=SC4[:, 0:n], in0=SC2[:, 0:n],
                                          in1=SC4[:, 0:n], op=ALU.add))
            P.v(lambda e: e.scalar_tensor_tensor(out=SC4[:, 0:n], in0=SC4[:, 0:n],
                                                 scalar=-16.0, in1=SC3[:, 0:n],
                                                 op0=ALU.add, op1=ALU.mult))
            o3 = out_bf16_ap.rearrange("p (b k) -> p b k", k=32)
            d3 = SC4[:, 0:n].rearrange("p (b k) -> p b k", k=32)
            P.v(lambda e: e.tensor_tensor(out=o3, in0=d3, in1=s3, op=ALU.mult),
                inc=('v', 1))

        def quant_ladder(src_ap, n, out_bf16_ap):
            nb = n // 32
            src3 = src_ap.rearrange("p (b k) -> p b k", k=32)
            quant_block_scales(src3, nb)
            inv3 = invblk[:, 0:nb].unsqueeze(2).broadcast_to([128, nb, 32])
            s3 = sblk[:, 0:nb].unsqueeze(2).broadcast_to([128, nb, 32])
            y3 = SC1[:, 0:n].rearrange("p (b k) -> p b k", k=32)
            P.v(lambda e: e.tensor_tensor(out=y3, in0=src3, in1=inv3, op=ALU.mult))
            y = SC1[:, 0:n]
            P.v(lambda e: e.tensor_tensor(out=SC2[:, 0:n], in0=y, in1=y, op=ALU.mult))
            P.v(lambda e: e.tensor_single_scalar(out=SC2[:, 0:n], in_=SC2[:, 0:n],
                                                 scalar=2.0 ** -12, op=ALU.max),
                inc=('v', 1))
            P.s(lambda e: e.activation(out=SC2[:, 0:n], in_=SC2[:, 0:n], func=AF.Ln,
                                       bias=cst[:, C_ZERO:C_ZERO + 1], scale=1.0),
                inc=('s', 1), waits=P.wv())
            P.v(lambda e: e.tensor_scalar(out=SC3[:, 0:n], in0=SC2[:, 0:n],
                                          scalar1=INV2LN2, scalar2=6.0,
                                          op0=ALU.mult, op1=ALU.add),
                waits=P.ws())
            P.v(lambda e: e.tensor_copy(SC2[:, 0:n], SC3[:, 0:n]))
            for K in (8.0, 4.0, 2.0, 1.0):
                P.v(lambda e, K=K: e.tensor_single_scalar(out=SC4[:, 0:n],
                                                          in_=SC2[:, 0:n], scalar=K,
                                                          op=ALU.is_ge))
                P.v(lambda e, K=K: e.scalar_tensor_tensor(out=SC2[:, 0:n],
                                                          in0=SC4[:, 0:n], scalar=-K,
                                                          in1=SC2[:, 0:n],
                                                          op0=ALU.mult, op1=ALU.add))
            P.v(lambda e: e.tensor_tensor(out=SC3[:, 0:n], in0=SC3[:, 0:n],
                                          in1=SC2[:, 0:n], op=ALU.subtract),
                inc=('v', 1))
            P.s(lambda e: e.activation(out=SC2[:, 0:n], in_=SC3[:, 0:n], func=AF.Exp,
                                       bias=cst[:, C_BISTEPH:C_BISTEPH + 1],
                                       scale=-LN2), waits=P.wv())
            P.s(lambda e: e.activation(out=SC3[:, 0:n], in_=SC3[:, 0:n], func=AF.Exp,
                                       bias=cst[:, C_BSTEPH:C_BSTEPH + 1], scale=LN2),
                inc=('s', 1))
            P.v(lambda e: e.tensor_tensor(out=SC1[:, 0:n], in0=SC1[:, 0:n],
                                          in1=SC2[:, 0:n], op=ALU.mult),
                waits=P.ws())
            P.v(lambda e: e.tensor_single_scalar(out=SC1[:, 0:n], in_=SC1[:, 0:n],
                                                 scalar=16.5, op=ALU.add))
            P.v(lambda e: e.tensor_copy(SC2[:, 0:n], SC1[:, 0:n]))
            for K in (16.0, 8.0, 4.0, 2.0, 1.0, 1.0):
                P.v(lambda e, K=K: e.tensor_single_scalar(out=SC4[:, 0:n],
                                                          in_=SC2[:, 0:n], scalar=K,
                                                          op=ALU.is_ge))
                P.v(lambda e, K=K: e.scalar_tensor_tensor(out=SC2[:, 0:n],
                                                          in0=SC4[:, 0:n], scalar=-K,
                                                          in1=SC2[:, 0:n],
                                                          op0=ALU.mult, op1=ALU.add))
            P.v(lambda e: e.tensor_tensor(out=SC1[:, 0:n], in0=SC1[:, 0:n],
                                          in1=SC2[:, 0:n], op=ALU.subtract))
            P.v(lambda e: e.scalar_tensor_tensor(out=SC1[:, 0:n], in0=SC1[:, 0:n],
                                                 scalar=-16.0, in1=SC3[:, 0:n],
                                                 op0=ALU.add, op1=ALU.mult))
            o3 = out_bf16_ap.rearrange("p (b k) -> p b k", k=32)
            d3 = SC1[:, 0:n].rearrange("p (b k) -> p b k", k=32)
            P.v(lambda e: e.tensor_tensor(out=o3, in0=d3, in1=s3, op=ALU.mult),
                inc=('v', 1))

        def layernorm_sc2(n=D):
            P.v(lambda e: e.tensor_reduce(out=tiny1[:, 0:1], in_=SC2[:, 0:n], axis=AX.X,
                                          op=ALU.add), waits=P.wdm())
            P.v(lambda e: e.tensor_single_scalar(out=tiny1[:, 0:1], in_=tiny1[:, 0:1],
                                                 scalar=1.0 / n, op=ALU.mult))
            P.v(lambda e: e.tensor_scalar(out=SC4[:, 0:n], in0=SC2[:, 0:n],
                                          scalar1=tiny1[:, 0:1], scalar2=0.0,
                                          op0=ALU.subtract, op1=ALU.add))
            P.v(lambda e: e.tensor_tensor(out=SC3[:, 0:n], in0=SC4[:, 0:n],
                                          in1=SC4[:, 0:n], op=ALU.mult))
            P.v(lambda e: e.tensor_reduce(out=tiny1[:, 1:2], in_=SC3[:, 0:n],
                                          axis=AX.X, op=ALU.add), inc=('v', 1))
            P.v(lambda e: e.tensor_scalar(out=tiny1[:, 4:5], in0=tiny1[:, 1:2],
                                          scalar1=1.0 / n, scalar2=EPS,
                                          op0=ALU.mult, op1=ALU.add))
            P.v(lambda e: e.tensor_copy(tiny1[:, 6:7], tiny1[:, 4:5]), inc=('v', 1))
            P.s(lambda e: e.activation(out=tiny1[:, 2:3], in_=tiny1[:, 4:5],
                                       func=AF.Sqrt, bias=cst[:, C_ZERO:C_ZERO + 1],
                                       scale=1.0),
                inc=('s', 1), waits=P.wv())
            P.v(lambda e: e.reciprocal(out=tiny1[:, 3:4], in_=tiny1[:, 2:3]),
                waits=P.ws())
            P.v(lambda e: e.tensor_tensor(out=tiny1[:, 5:6], in0=tiny1[:, 3:4],
                                          in1=tiny1[:, 3:4], op=ALU.mult))
            P.v(lambda e: e.tensor_tensor(out=tiny1[:, 5:6], in0=tiny1[:, 5:6],
                                          in1=tiny1[:, 4:5], op=ALU.mult))
            P.v(lambda e: e.tensor_scalar(out=tiny1[:, 5:6], in0=tiny1[:, 5:6],
                                          scalar1=-0.5, scalar2=1.5,
                                          op0=ALU.mult, op1=ALU.add))
            P.v(lambda e: e.tensor_tensor(out=tiny1[:, 3:4], in0=tiny1[:, 3:4],
                                          in1=tiny1[:, 5:6], op=ALU.mult))
            P.v(lambda e: e.tensor_scalar(out=SC2[:, 0:n], in0=SC4[:, 0:n],
                                          scalar1=tiny1[:, 3:4], scalar2=0.0,
                                          op0=ALU.mult, op1=ALU.add),
                inc=('v', 1))

        BIGA3 = BIGA[:].rearrange("p (b t) -> p b t", b=16)
        BIGAq = BIGA[:].rearrange("p (b t) -> p b t", b=64)
        BIGCg = BIGC[:, 0:8192].rearrange("p (b t) -> p b t", b=16)
        BIGCu = BIGC[:, 8192:16384].rearrange("p (b t) -> p b t", b=16)
        BIGC64 = BIGC[:, 0:8192].rearrange("p (b t) -> p b t", b=64)
        SST3 = SST[:].rearrange("p (h d) -> p h d", h=16)
        SB163 = SB16[:].rearrange("p (h d) -> p h d", h=16)

        for j, val in enumerate(CONST_VALS):
            P.v(lambda e, j=j, val=val: e.memset(cst[:, j:j + 1], val))
        P.dma(wnt[:], wnormt[:])
        P.dma(nbg[:], bgk2l[:])
        P.dma(maskt[:], maskt_d[:])
        P.dma(ident[:], ident_d[:])
        P.v(lambda e: e.tensor_single_scalar(out=nbg[:], in_=nbg[:], scalar=-1.0,
                                             op=ALU.mult), inc=('v', 1),
            waits=P.wdm())
        P.v(lambda e: e.memset(SST[:], 0.0))
        P.v(lambda e: e.memset(SB16[:], 0.0), inc=('v', 1))

        for tt in range(NT):
            P.dma(SC2[:], x_in[tt * 128:(tt + 1) * 128, :], waits=P.wv())
            layernorm_sc2()
            quant_cheap(SC2[:], D, SCB1[:])
            P.dmat(BIGA3[:, :, tt * 128:(tt + 1) * 128], SCB1[:], waits=P.wv())

        for ic in range(NIC):
            for (wsrc, wdst3, wflat, whbm) in (
                    (gate_w, BIGCg, BIGC[:, 0:8192], dqwTg),
                    (up_w, BIGCu, BIGC[:, 8192:16384], dqwTu)):
                for a in range(4):
                    rows = slice(ic * 512 + a * 128, ic * 512 + (a + 1) * 128)
                    P.dma(SC2[:], wsrc[rows, :], waits=P.wv() + P.wt())
                    quant_cheap(SC2[:], D, SCB1[:])
                    P.dmat(wdst3[:, :, a * 128:(a + 1) * 128], SCB1[:], waits=P.wv())
                P.dma(whbm[ic, :, :], wflat, waits=P.wdm())
            for tt in range(NT):
                dmw = P.wdm()
                vw = P.wv()
                sw = P.ws()
                for kb in range(NKB):
                    P.t(lambda e, kb=kb, tt=tt: e.matmul(
                        PB[0][:], lhsT=BIGA3[:, kb, tt * 128:(tt + 1) * 128],
                        rhs=BIGCg[:, kb, :], start=(kb == 0), stop=(kb == NKB - 1)),
                        waits=(dmw + vw + sw) if kb == 0 else ())
                for kb in range(NKB):
                    P.t(lambda e, kb=kb, tt=tt: e.matmul(
                        PB[1][:], lhsT=BIGA3[:, kb, tt * 128:(tt + 1) * 128],
                        rhs=BIGCu[:, kb, :], start=(kb == 0), stop=(kb == NKB - 1)),
                        inc=('t', 1) if kb == NKB - 1 else None)
                P.s(lambda e: e.activation(out=SC3[:, 0:512], in_=PB[0][:], func=AF.Silu,
                                           bias=cst[:, C_ZERO:C_ZERO + 1], scale=1.0),
                    inc=('s', 1), waits=P.wt())
                P.v(lambda e: e.tensor_tensor(out=SC2[:, 0:512], in0=SC3[:, 0:512],
                                              in1=PB[1][:], op=ALU.mult),
                    inc=('v', 1), waits=P.ws())
                quant_ladder(SC2[:, 0:512], 512, SCB1[:, 0:512])
                P.dmat(SCB2[:, 0:512].rearrange("p (b t) -> p b t", b=4),
                       SCB1[:, 0:512], waits=P.wv())
                P.dma(dqhT[tt, :, ic * 512:(ic + 1) * 512], SCB2[:, 0:512],
                      waits=P.wdm())

        for oc in range(4):
            for a in range(4):
                rows = slice(oc * 512 + a * 128, oc * 512 + (a + 1) * 128)
                for half in range(4):
                    cols = slice(half * 2048, (half + 1) * 2048)
                    P.dma(SC2[:], down_w[rows, cols], waits=P.wv() + P.wt())
                    quant_cheap(SC2[:], D, SCB1[:])
                    P.dmat(BIGAq[:, half * 16:(half + 1) * 16, a * 128:(a + 1) * 128],
                           SCB1[:], waits=P.wv())
            P.dma(dqwTd[oc, :, :], BIGA[:], waits=P.wdm())
            for tt in range(NT):
                P.dma(BIGC[:, 0:8192], dqhT[tt, :, 0:8192], waits=P.wt() + P.wdm())
                dmw = P.wdm()
                vw = P.wv()
                for kb in range(NKI):
                    P.t(lambda e, kb=kb: e.matmul(
                        PB[2][:], lhsT=BIGC64[:, kb, :], rhs=BIGAq[:, kb, :],
                        start=(kb == 0), stop=(kb == NKI - 1)),
                        inc=('t', 1) if kb == NKI - 1 else None,
                        waits=(dmw + vw) if kb == 0 else ())
                P.dma(SC3[:, 0:512],
                      x_in[tt * 128:(tt + 1) * 128, oc * 512:(oc + 1) * 512],
                      waits=P.wv())
                P.v(lambda e: e.scalar_tensor_tensor(out=SC4[:, 0:512], in0=PB[2][:],
                                                     scalar=0.5, in1=SC3[:, 0:512],
                                                     op0=ALU.mult, op1=ALU.add),
                    inc=('v', 1), waits=P.wt() + P.wdm())
                P.dma(x1d[tt, :, oc * 512:(oc + 1) * 512], SC4[:, 0:512], waits=P.wv())

        for (wsrc, wdst, nkb, ncols) in ((wq, wq16, NKB, D), (wk, wk16, NKB, D),
                                         (wv, wv16, NKB, 1024), (wg, wg16, NKB, 1024),
                                         (wgk1p, wgk1p16, NKB, 128)):
            for kb in range(nkb):
                P.dma(SC2[:, 0:ncols], wsrc[kb * 128:(kb + 1) * 128, :],
                      waits=P.wv() + P.wdm())
                P.v(lambda e, n=ncols: e.tensor_copy(SCB1[:, 0:n], SC2[:, 0:n]),
                    inc=('v', 1), waits=P.wdm())
                P.dma(wdst[kb, :, :], SCB1[:, 0:ncols], waits=P.wv())
        P.dma(SC2[:], wgk2p[:], waits=P.wv() + P.wdm())
        P.v(lambda e: e.tensor_copy(SCB1[:], SC2[:]), inc=('v', 1), waits=P.wdm())
        P.dma(wgk2p16[0, :, :], SCB1[:], waits=P.wv())
        for vb in range(8):
            P.dma(SC2[:], wo[vb * 128:(vb + 1) * 128, :], waits=P.wv() + P.wdm())
            P.v(lambda e, vb=vb: e.tensor_scalar(out=SCB1[:], in0=SC2[:],
                                                 scalar1=wnt[:, vb:vb + 1], scalar2=0.0,
                                                 op0=ALU.mult, op1=ALU.add),
                inc=('v', 1), waits=P.wdm())
            P.dma(woN16[vb, :, :], SCB1[:], waits=P.wv())

        x1nT3 = BIGA[:, 0:8192].rearrange("p (b t) -> p b t", b=16)
        kg3 = BIGA[:, 8192:16384].rearrange("p (b t) -> p b t", b=16)
        qg3 = BIGA[:, 16384:24576].rearrange("p (b t) -> p b t", b=16)
        wbuf = BIGA[:, 24576:32768]
        wbuf3 = wbuf.rearrange("p (b t) -> p b t", b=16)
        vg3 = BIGC[:, 0:4096].rearrange("p (c t) -> p c t", c=4)
        sg3 = BIGC[:, 4096:8192].rearrange("p (c t) -> p c t", c=4)
        om3 = BIGC[:, 8192:12288].rearrange("p (c t) -> p c t", c=4)
        omT3 = BIGC[:, 12288:16384].rearrange("p (b t) -> p b t", b=8)
        Bg3 = BIGB[:].rearrange("p (b t) -> p b t", b=16)
        qe_t = SCB1[:, 0:512]
        kd_t = SCB1[:, 512:1024]
        kw_t = SCB1[:, 1024:1536]
        kwT_t = SCB1[:, 1536:2048]
        amT_t = SCB2[:, 0:128]

        for g in range(NG):
            own = g >= 2
            for j in range(4):
                tt = g * 4 + j
                P.dma(SC2[:], x1d[tt, :, :], waits=P.wv())
                layernorm_sc2()
                P.v(lambda e: e.tensor_copy(SCB1[:], SC2[:]), inc=('v', 1))
                P.dmat(x1nT3[:, :, j * 128:(j + 1) * 128], SCB1[:], waits=P.wv())
            dmx = P.wdm()
            for (w16, dst3, do) in ((wk16, kg3, True), (wq16, qg3, own)):
                if not do:
                    continue
                for fb in range(16):
                    P.dma(SCB2[:].rearrange("p (b t) -> p b t", t=128),
                          w16[:, :, fb * 128:(fb + 1) * 128].rearrange("b p t -> p b t"),
                          waits=P.wv() + P.wt())
                    dmw = P.wdm()
                    vw = P.wv()
                    for kb in range(NKB):
                        P.t(lambda e, kb=kb: e.matmul(
                            PB[3][:], lhsT=SCB2[:, kb * 128:(kb + 1) * 128],
                            rhs=x1nT3[:, kb, :], start=(kb == 0), stop=(kb == NKB - 1)),
                            inc=('t', 1) if kb == NKB - 1 else None,
                            waits=(dmw + dmx + vw) if kb == 0 else ())
                    P.v(lambda e, fb=fb, d=dst3: e.tensor_copy(d[:, fb, :], PB[3][:]),
                        inc=('v', 1), waits=P.wt())
            P.dma(SCB2[:].rearrange("p (b t) -> p b t", t=128),
                  wgk1p16[:, :, :].rearrange("b p t -> p b t"),
                  waits=P.wv() + P.wt())
            dmw = P.wdm()
            for kb in range(NKB):
                P.t(lambda e, kb=kb: e.matmul(
                    PB[3][:], lhsT=SCB2[:, kb * 128:(kb + 1) * 128],
                    rhs=x1nT3[:, kb, :], start=(kb == 0), stop=(kb == NKB - 1)),
                    inc=('t', 1) if kb == NKB - 1 else None,
                    waits=(dmw + dmx + P.wv()) if kb == 0 else ())
            P.v(lambda e: e.tensor_copy(t1sb[:], PB[3][:]), inc=('v', 1), waits=P.wt())
            P.dma(SCB2[:], wgk2p16[0, :, :], waits=P.wt())
            dmw = P.wdm()
            for fb in range(16):
                P.t(lambda e, fb=fb: e.matmul(
                    PB[3][:], lhsT=SCB2[:, fb * 128:(fb + 1) * 128], rhs=t1sb[:],
                    start=True, stop=True),
                    inc=('t', 1), waits=dmw + P.wv() + P.ws())
                P.s(lambda e, fb=fb: e.activation(out=SC3[:, 0:512], in_=PB[3][:],
                                                  func=AF.Exp, bias=nbg[:, fb:fb + 1],
                                                  scale=-1.0),
                    waits=P.wt())
                P.s(lambda e: e.activation(out=SC3[:, 0:512], in_=SC3[:, 0:512],
                                           func=AF.Ln, bias=cst[:, C_ONE:C_ONE + 1],
                                           scale=1.0),
                    inc=('s', 1))
                for cc in range(4):
                    P.v(lambda e, fb=fb, cc=cc: e.tensor_tensor_scan(
                        out=Bg3[:, fb, cc * 128:(cc + 1) * 128],
                        data0=SC3[:, cc * 128:(cc + 1) * 128],
                        data1=SC3[:, cc * 128:(cc + 1) * 128],
                        initial=0.0, op0=ALU.add, op1=ALU.bypass),
                        inc=('v', 1) if cc == 3 else None,
                        waits=P.ws() if cc == 0 else ())
            for (w16, dst3, do, silu) in ((wv16, vg3, True, False),
                                          (wg16, sg3, own, True)):
                if not do:
                    continue
                for vc in range(2):
                    P.dma(wbuf.rearrange("p (b t) -> p b t", b=16),
                          w16[:, :, vc * 512:(vc + 1) * 512].rearrange("b p t -> p b t"),
                          waits=P.wv() + P.ws() + P.wt())
                    dmw = P.wdm()
                    for tc in range(4):
                        vw = P.wv()
                        sw = P.ws()
                        for kb in range(NKB):
                            P.t(lambda e, kb=kb, tc=tc: e.matmul(
                                PB[3][:], lhsT=x1nT3[:, kb, tc * 128:(tc + 1) * 128],
                                rhs=wbuf[:, kb * 512:(kb + 1) * 512],
                                start=(kb == 0), stop=(kb == NKB - 1)),
                                inc=('t', 1) if kb == NKB - 1 else None,
                                waits=(dmw + dmx + vw + sw) if kb == 0 else ())
                        if silu:
                            P.s(lambda e, tc=tc, vc=vc, d=dst3: e.activation(
                                out=d[:, tc, vc * 512:(vc + 1) * 512], in_=PB[3][:],
                                func=AF.Silu, bias=cst[:, C_ZERO:C_ZERO + 1],
                                scale=1.0),
                                inc=('s', 1), waits=P.wt())
                        else:
                            P.v(lambda e, tc=tc, vc=vc, d=dst3: e.tensor_copy(
                                d[:, tc, vc * 512:(vc + 1) * 512], PB[3][:]),
                                inc=('v', 1), waits=P.wt())
            for tc in range(4):
                csl = slice(tc * 128, (tc + 1) * 128)
                for h in range(H):
                    vw_prev = P.wv() + P.wt() + P.wdm()
                    for d4 in range(4):
                        fb = h * 4 + d4
                        P.s(lambda e, fb=fb, tc=tc, d4=d4: e.activation(
                            out=tiny2[:, d4:d4 + 1],
                            in_=Bg3[:, fb, tc * 128 + 127:tc * 128 + 128],
                            func=AF.Exp, bias=cst[:, C_ZERO:C_ZERO + 1],
                            scale=-1.0 / 16.0),
                            inc=('s', 1), waits=vw_prev if d4 == 0 else ())
                    for d4 in range(4):
                        fb = h * 4 + d4
                        dsl = slice(d4 * 128, (d4 + 1) * 128)
                        P.s(lambda e, fb=fb, csl=csl, d4=d4: e.activation(
                            out=SC3[:, d4 * 128:(d4 + 1) * 128], in_=Bg3[:, fb, csl],
                            func=AF.Exp, bias=cst[:, C_ZERO:C_ZERO + 1],
                            scale=1.0 / 16.0),
                            inc=('s', 1), waits=vw_prev if d4 == 0 else ())
                        P.v(lambda e, fb=fb, csl=csl, dsl=dsl, d4=d4: e.tensor_tensor(
                            out=kd_t[:, dsl], in0=kg3[:, fb, csl],
                            in1=SC3[:, d4 * 128:(d4 + 1) * 128], op=ALU.mult),
                            waits=P.ws())
                        P.v(lambda e, dsl=dsl, d4=d4: e.tensor_scalar(
                            out=kw_t[:, dsl], in0=kd_t[:, dsl],
                            scalar1=tiny2[:, d4:d4 + 1], scalar2=0.0,
                            op0=ALU.mult, op1=ALU.add))
                        if own:
                            P.s(lambda e, fb=fb, csl=csl, d4=d4: e.activation(
                                out=SC3[:, 1024 + d4 * 128:1024 + (d4 + 1) * 128],
                                in_=Bg3[:, fb, csl], func=AF.Exp,
                                bias=cst[:, C_QSC:C_QSC + 1], scale=-1.0 / 16.0),
                                inc=('s', 1))
                            P.v(lambda e, fb=fb, csl=csl, dsl=dsl, d4=d4: e.tensor_tensor(
                                out=qe_t[:, dsl], in0=qg3[:, fb, csl],
                                in1=SC3[:, 1024 + d4 * 128:1024 + (d4 + 1) * 128],
                                op=ALU.mult),
                                waits=P.ws())
                    P.v(lambda e: e.tensor_copy(tiny1[:, 7:8], tiny1[:, 0:1]),
                        inc=('v', 1))
                    if own:
                        for d4 in range(4):
                            dsl = slice(d4 * 128, (d4 + 1) * 128)
                            P.t(lambda e, dsl=dsl, d4=d4: e.matmul(
                                PB[4][:, 0:128], lhsT=kd_t[:, dsl], rhs=qe_t[:, dsl],
                                start=(d4 == 0), stop=(d4 == 3)),
                                inc=('t', 1) if d4 == 3 else None,
                                waits=P.wv() if d4 == 0 else ())
                        P.v(lambda e: e.tensor_tensor(out=amT_t, in0=PB[4][:, 0:128],
                                                      in1=maskt[:], op=ALU.mult),
                            inc=('v', 1), waits=P.wt())
                        P.t(lambda e, tc=tc, h=h: e.matmul(
                            PB[5][:, 0:256], lhsT=amT_t,
                            rhs=vg3[:, tc, h * 256:(h + 1) * 256],
                            start=True, stop=False), waits=P.wv())
                        for d4 in range(4):
                            dsl = slice(d4 * 128, (d4 + 1) * 128)
                            P.t(lambda e, h=h, d4=d4, dsl=dsl: e.matmul(
                                PB[5][:, 0:256], lhsT=qe_t[:, dsl],
                                rhs=SB163[:, h * 4 + d4, :], start=False,
                                stop=(d4 == 3)),
                                inc=('t', 1) if d4 == 3 else None)
                    P.dmat(kwT_t.rearrange("p (b t) -> p b t", b=4), kw_t,
                           waits=P.wv())
                    kwT3 = kwT_t.rearrange("p (b t) -> p b t", b=4)
                    Uslices = [PB[6][:, 0:256], PB[6][:, 256:512],
                               PB[7][:, 0:256], PB[7][:, 256:512]]
                    dmk = P.wdm()
                    for d4 in range(4):
                        P.t(lambda e, d4=d4, tc=tc, h=h, kwT3=kwT3: e.matmul(
                            Uslices[d4], lhsT=kwT3[:, d4, :],
                            rhs=vg3[:, tc, h * 256:(h + 1) * 256],
                            start=True, stop=True),
                            inc=('t', 1) if d4 == 3 else None,
                            waits=dmk if d4 == 0 else ())
                    for d4 in range(4):
                        P.v(lambda e, h=h, d4=d4: e.tensor_scalar(
                            out=SST3[:, h * 4 + d4, :], in0=SST3[:, h * 4 + d4, :],
                            scalar1=tiny2[:, d4:d4 + 1], scalar2=0.0,
                            op0=ALU.mult, op1=ALU.add),
                            waits=P.wt() if d4 == 0 else ())
                        P.v(lambda e, h=h, d4=d4: e.tensor_tensor(
                            out=SST3[:, h * 4 + d4, :], in0=SST3[:, h * 4 + d4, :],
                            in1=Uslices[d4], op=ALU.add))
                        P.v(lambda e, h=h, d4=d4: e.tensor_copy(
                            SB163[:, h * 4 + d4, :], SST3[:, h * 4 + d4, :]),
                            inc=('v', 1) if d4 == 3 else None)
                    if own:
                        P.v(lambda e: e.tensor_copy(SC3[:, 0:256], PB[5][:, 0:256]))
                        P.v(lambda e: e.tensor_tensor(out=SC3[:, 256:512],
                                                      in0=SC3[:, 0:256],
                                                      in1=SC3[:, 0:256],
                                                      op=ALU.mult))
                        P.v(lambda e: e.tensor_reduce(out=tiny1[:, 4:5],
                                                      in_=SC3[:, 256:512], axis=AX.X,
                                                      op=ALU.add), inc=('v', 1))
                        P.v(lambda e: e.tensor_scalar(out=tiny1[:, 4:5],
                                                      in0=tiny1[:, 4:5],
                                                      scalar1=1.0 / 256.0, scalar2=EPS,
                                                      op0=ALU.mult, op1=ALU.add))
                        P.v(lambda e: e.tensor_copy(tiny1[:, 6:7], tiny1[:, 4:5]), inc=('v', 1))
                        P.s(lambda e: e.activation(out=tiny1[:, 5:6], in_=tiny1[:, 4:5],
                                                   func=AF.Sqrt,
                                                   bias=cst[:, C_ZERO:C_ZERO + 1],
                                                   scale=1.0),
                            inc=('s', 1), waits=P.wv())
                        P.v(lambda e: e.reciprocal(out=tiny1[:, 6:7], in_=tiny1[:, 5:6]),
                            waits=P.ws())
                        P.v(lambda e: e.tensor_tensor(out=tiny1[:, 7:8],
                                                      in0=tiny1[:, 6:7],
                                                      in1=tiny1[:, 6:7], op=ALU.mult))
                        P.v(lambda e: e.tensor_tensor(out=tiny1[:, 7:8],
                                                      in0=tiny1[:, 7:8],
                                                      in1=tiny1[:, 4:5], op=ALU.mult))
                        P.v(lambda e: e.tensor_scalar(out=tiny1[:, 7:8],
                                                      in0=tiny1[:, 7:8],
                                                      scalar1=-0.5, scalar2=1.5,
                                                      op0=ALU.mult, op1=ALU.add))
                        P.v(lambda e: e.tensor_tensor(out=tiny1[:, 6:7],
                                                      in0=tiny1[:, 6:7],
                                                      in1=tiny1[:, 7:8], op=ALU.mult))
                        P.v(lambda e: e.tensor_scalar(out=SC3[:, 0:256],
                                                      in0=SC3[:, 0:256],
                                                      scalar1=tiny1[:, 6:7],
                                                      scalar2=0.0,
                                                      op0=ALU.mult, op1=ALU.add))
                        P.v(lambda e, tc=tc, h=h: e.tensor_tensor(
                            out=om3[:, tc, h * 256:(h + 1) * 256], in0=SC3[:, 0:256],
                            in1=sg3[:, tc, h * 256:(h + 1) * 256], op=ALU.mult),
                            inc=('v', 1))
            if own:
                for tc in range(4):
                    P.dmat(omT3[:, :, tc * 128:(tc + 1) * 128],
                           BIGC[:, 8192 + tc * 1024:8192 + (tc + 1) * 1024],
                           waits=P.wv())
                for oc in range(4):
                    P.dma(wbuf[:, 0:4096].rearrange("p (b t) -> p b t", b=8),
                          woN16[:, :, oc * 512:(oc + 1) * 512].rearrange("b p t -> p b t"),
                          waits=P.wv() + P.wt())
                    dmw = P.wdm()
                    for tc in range(4):
                        tt = g * 4 + tc
                        vw = P.wv()
                        for vb in range(8):
                            P.t(lambda e, vb=vb, tc=tc: e.matmul(
                                PB[3][:], lhsT=omT3[:, vb, tc * 128:(tc + 1) * 128],
                                rhs=wbuf[:, vb * 512:(vb + 1) * 512],
                                start=(vb == 0), stop=(vb == 7)),
                                inc=('t', 1) if vb == 7 else None,
                                waits=(dmw + vw) if vb == 0 else ())
                        P.dma(SC3[:, 0:512], x1d[tt, :, oc * 512:(oc + 1) * 512],
                              waits=P.wv())
                        P.v(lambda e: e.tensor_tensor(out=SC4[:, 0:512], in0=PB[3][:],
                                                      in1=SC3[:, 0:512], op=ALU.add),
                            inc=('v', 1), waits=P.wt() + P.wdm())
                        P.dma(x2d[tt - 8, :, oc * 512:(oc + 1) * 512], SC4[:, 0:512],
                              waits=P.wv())

        x2T3 = BIGA[:, 0:16384].rearrange("p (b t) -> p b t", b=16)
        for tt in range(NT_OWN):
            P.dma(SC2[:], x2d[tt, :, :], waits=P.wv() + P.wt())
            layernorm_sc2()
            quant_cheap(SC2[:], D, SCB1[:])
            P.dmat(x2T3[:, :, tt * 128:(tt + 1) * 128], SCB1[:], waits=P.wv())
        for ic in range(NIC):
            P.dma(BIGC[:, 0:8192], dqwTg[ic, :, :], waits=P.wt() + P.wdm())
            P.dma(BIGC[:, 8192:16384], dqwTu[ic, :, :])
            dmw = P.wdm()
            for tt in range(NT_OWN):
                vw = P.wv()
                sw = P.ws()
                for kb in range(NKB):
                    P.t(lambda e, kb=kb, tt=tt: e.matmul(
                        PB[0][:], lhsT=x2T3[:, kb, tt * 128:(tt + 1) * 128],
                        rhs=BIGCg[:, kb, :], start=(kb == 0), stop=(kb == NKB - 1)),
                        waits=(dmw + vw + sw) if kb == 0 else ())
                for kb in range(NKB):
                    P.t(lambda e, kb=kb, tt=tt: e.matmul(
                        PB[1][:], lhsT=x2T3[:, kb, tt * 128:(tt + 1) * 128],
                        rhs=BIGCu[:, kb, :], start=(kb == 0), stop=(kb == NKB - 1)),
                        inc=('t', 1) if kb == NKB - 1 else None)
                P.s(lambda e: e.activation(out=SC3[:, 0:512], in_=PB[0][:], func=AF.Silu,
                                           bias=cst[:, C_ZERO:C_ZERO + 1], scale=1.0),
                    inc=('s', 1), waits=P.wt())
                P.v(lambda e: e.tensor_tensor(out=SC2[:, 0:512], in0=SC3[:, 0:512],
                                              in1=PB[1][:], op=ALU.mult),
                    inc=('v', 1), waits=P.ws())
                quant_ladder(SC2[:, 0:512], 512, SCB1[:, 0:512])
                P.dmat(SCB2[:, 0:512].rearrange("p (b t) -> p b t", b=4),
                       SCB1[:, 0:512], waits=P.wv())
                P.dma(dqhT[tt, :, ic * 512:(ic + 1) * 512], SCB2[:, 0:512],
                      waits=P.wdm())
        for oc in range(4):
            P.dma(BIGA[:], dqwTd[oc, :, :], waits=P.wt() + P.wdm())
            for tt in range(NT_OWN):
                P.dma(BIGC[:, 0:8192], dqhT[tt, :, 0:8192], waits=P.wt() + P.wdm())
                dmw = P.wdm()
                vw = P.wv()
                for kb in range(NKI):
                    P.t(lambda e, kb=kb: e.matmul(
                        PB[2][:], lhsT=BIGC64[:, kb, :], rhs=BIGAq[:, kb, :],
                        start=(kb == 0), stop=(kb == NKI - 1)),
                        inc=('t', 1) if kb == NKI - 1 else None,
                        waits=(dmw + vw) if kb == 0 else ())
                P.dma(SC3[:, 0:512], x2d[tt, :, oc * 512:(oc + 1) * 512], waits=P.wv())
                P.v(lambda e: e.scalar_tensor_tensor(out=SC4[:, 0:512], in0=PB[2][:],
                                                     scalar=0.5, in1=SC3[:, 0:512],
                                                     op0=ALU.mult, op1=ALU.add),
                    inc=('v', 1), waits=P.wt() + P.wdm())
                P.dma(y_out[tt * 128:(tt + 1) * 128, oc * 512:(oc + 1) * 512],
                      SC4[:, 0:512], waits=P.wv())

        final_dm = P.wdm()
        P.op('sync', lambda e: None, waits=final_dm)

        with nc.Block() as block:
            for eng_name, starter in (('sync', block.sync), ('vector', block.vector),
                                      ('scalar', block.scalar), ('tensor', block.tensor)):
                def make_emit(eng_name=eng_name):
                    def emit(eng):
                        last_wait = {}
                        for (en, fn, inc, waits) in P.items:
                            if en != eng_name:
                                continue
                            for (sname, val) in waits:
                                if val <= 0 or last_wait.get(sname, -1) >= val:
                                    continue
                                eng.wait_ge(sems[sname], val)
                                last_wait[sname] = val
                            inst = fn(eng)
                            if inc is not None:
                                inst.then_inc(sems[inc[0]], inc[1])
                            if eng_name in ('vector', 'scalar'):
                                eng.drain()
                    return emit
                starter(make_emit())
    return nc


_NC_CACHE = []


def kernel(L0_attn_x, L0_attn_Wq, L0_attn_Wk, L0_attn_Wv, L0_attn_Wgk1, L0_attn_Wgk2,
           L0_attn_bgk2, L0_attn_Wg, L0_attn_Wnorm, L0_attn_Wo, L0_ffn_gate_w,
           L0_ffn_up_w, L0_ffn_down_w):
    x = np.ascontiguousarray(np.asarray(L0_attn_x, dtype=np.float32))
    wq_h = np.ascontiguousarray(np.asarray(L0_attn_Wq, dtype=np.float32))
    wk_h = np.ascontiguousarray(np.asarray(L0_attn_Wk, dtype=np.float32))
    wv_h = np.ascontiguousarray(np.asarray(L0_attn_Wv, dtype=np.float32))
    wgk1 = np.asarray(L0_attn_Wgk1, dtype=np.float32)
    wgk2 = np.asarray(L0_attn_Wgk2, dtype=np.float32)
    bgk2 = np.asarray(L0_attn_bgk2, dtype=np.float32)
    wg_h = np.ascontiguousarray(np.asarray(L0_attn_Wg, dtype=np.float32))
    wnorm = np.asarray(L0_attn_Wnorm, dtype=np.float32)
    wo_h = np.ascontiguousarray(np.asarray(L0_attn_Wo, dtype=np.float32))
    gate_h = np.ascontiguousarray(np.asarray(L0_ffn_gate_w, dtype=np.float32))
    up_h = np.ascontiguousarray(np.asarray(L0_ffn_up_w, dtype=np.float32))
    down_h = np.ascontiguousarray(np.asarray(L0_ffn_down_w, dtype=np.float32))

    wgk1p = np.zeros((2048, 128), dtype=np.float32)
    wgk1p[:, 0:16] = wgk1
    wgk2p = np.zeros((128, 2048), dtype=np.float32)
    wgk2p[0:16, :] = wgk2
    bgk2l = np.ascontiguousarray(bgk2.reshape(16, 128).T)
    wnormt = np.ascontiguousarray(np.tile(wnorm, 4).reshape(8, 128).T)
    s_idx = np.arange(128)[:, None]
    q_idx = np.arange(128)[None, :]
    maskt = np.ascontiguousarray((s_idx <= q_idx).astype(np.float32))
    ident = np.ascontiguousarray(np.eye(128, dtype=np.float32).astype(ml_dtypes.bfloat16))

    in_maps = []
    for c in range(8):
        b, hf = c // 2, c % 2
        pre = np.zeros((1024, 2048), np.float32) if hf == 0 else x[b, 0:1024]
        own = x[b, hf * 1024:(hf + 1) * 1024]
        x_c = np.ascontiguousarray(np.concatenate([pre, own], axis=0))
        in_maps.append({
            "x_in": x_c, "wq": wq_h, "wk": wk_h, "wv": wv_h, "wgk1p": wgk1p,
            "wgk2p": wgk2p, "bgk2l": bgk2l, "wg": wg_h, "wnormt": wnormt, "wo": wo_h,
            "gate_w": gate_h, "up_w": up_h, "down_w": down_h,
            "maskt": maskt, "ident": ident,
        })

    if not _NC_CACHE:
        _NC_CACHE.append(build_program())
    nc = _NC_CACHE[0]
    res = run_bass_kernel_spmd(nc, in_maps, core_ids=list(range(8)))
    out = np.empty((4, 2048, 2048), dtype=np.float32)
    for c in range(8):
        b, hf = c // 2, c % 2
        out[b, hf * 1024:(hf + 1) * 1024, :] = res.results[c]["y_out"]
    return out
